# version 01
# baseline (speedup 1.0000x reference)
import numpy as np
import ml_dtypes
import concourse.bass as bass
import concourse.mybir as mybir
from concourse.bass_utils import run_bass_kernel_spmd

F32 = mybir.dt.float32
BF16 = mybir.dt.bfloat16
I16 = mybir.dt.int16
U16 = mybir.dt.uint16
I8 = mybir.dt.int8
AF = mybir.ActivationFunctionType
ALU = mybir.AluOpType

P = 128
D = 1024
DC = 8
T = 4096
TT = 512
NTILE = T // TT
I = 3072
IC = 24
QMAX = 32767.0
EPS_LN = 1e-5


class Sched:
    ENGINES = ("sync", "gpsimd", "vector", "scalar", "tensor")

    def __init__(self):
        self.q = {e: [] for e in self.ENGINES}
        self.cnt = {}

    def op(self, eng, fn):
        self.q[eng].append(["inst", fn])

    def raw(self, eng, fn):
        self.q[eng].append(["raw", fn])

    def inc(self, sem_name, n=1):
        self.cnt[sem_name] = self.cnt.get(sem_name, 0) + n
        return self.cnt[sem_name]

    def val(self, sem_name):
        return self.cnt.get(sem_name, 0)


def build(dbg=0, parts=7):
    nc = bass.Bass("TRN2", target_bir_lowering=False)

    xT = nc.dram_tensor("xT", [D, T], F32, kind="ExternalInput").ap()
    wg_h = nc.dram_tensor("wg", [I, D], F32, kind="ExternalInput").ap()
    wu_h = nc.dram_tensor("wu", [I, D], F32, kind="ExternalInput").ap()
    wd_h = nc.dram_tensor("wd", [D, I], F32, kind="ExternalInput").ap()
    wqT = nc.dram_tensor("wqT", [D, D], F32, kind="ExternalInput").ap()
    wkT = nc.dram_tensor("wkT", [D, D], F32, kind="ExternalInput").ap()
    wvT = nc.dram_tensor("wvT", [D, D], F32, kind="ExternalInput").ap()
    out = nc.dram_tensor("out", [D, T], F32, kind="ExternalOutput").ap()

    dbg_outs = {}
    if dbg >= 1:
        for nm, shp, dt in [("dbg_gf", [P, DC], F32),
                            ("dbg_x1", [P, DC * T], F32),
                            ("dbg_ln2", [P, DC * T], BF16),
                            ("dbg_scal", [1, 16], F32)]:
            dbg_outs[nm] = nc.dram_tensor(nm, shp, dt, kind="ExternalOutput").ap()

    s_x1 = nc.dram_tensor("s_x1", [D, T], F32, kind="Internal").ap()
    s_ln2 = nc.dram_tensor("s_ln2", [D, T], BF16, kind="Internal").ap()
    s_gate = nc.dram_tensor("s_gate", [I, T], BF16, kind="Internal").ap()
    s_up = nc.dram_tensor("s_up", [I, T], BF16, kind="Internal").ap()
    s_G = nc.dram_tensor("s_G", [I, T], BF16, kind="Internal").ap()
    s_c = nc.dram_tensor("s_c", [I, T], BF16, kind="Internal").ap()
    s_wgT = nc.dram_tensor("s_wgT", [DC, I, P], BF16, kind="Internal").ap()
    s_wuT = nc.dram_tensor("s_wuT", [DC, I, P], BF16, kind="Internal").ap()
    s_wdT = nc.dram_tensor("s_wdT", [IC, D, P], BF16, kind="Internal").ap()
    s_scal = nc.dram_tensor("s_scal", [256], F32, kind="Internal").ap()
    cc_in = nc.dram_tensor("cc_in", [8], F32, kind="Internal").ap()
    cc_out = nc.dram_tensor("cc_out", [64], F32, kind="Internal").ap()

    xT3 = xT.rearrange("(c p) t -> p c t", p=P)
    ln2h3 = s_ln2.rearrange("(c p) t -> p c t", p=P)
    sg3 = s_gate.rearrange("(c p) t -> p c t", p=P)
    su3 = s_up.rearrange("(c p) t -> p c t", p=P)
    sG3 = s_G.rearrange("(c p) t -> p c t", p=P)
    sc3 = s_c.rearrange("(c p) t -> p c t", p=P)
    x13 = s_x1.rearrange("(c p) t -> p c t", p=P)
    out3 = out.rearrange("(c p) t -> p c t", p=P)

    S = Sched()
    galloc = []

    def sb(name, shape, dt):
        g = nc.sbuf_tensor(name, shape, dt)
        h = g.__enter__()
        galloc.append((name, g))
        return h

    def sbfree(*names):
        ns = set(names)
        assert len(ns) == len(names)
        suffix = [n for n, _ in galloc[-len(names):]]
        assert set(suffix) == ns, f"non-suffix free: {names} vs stack {suffix}"
        for _ in range(len(names)):
            n, g = galloc.pop()
            g.__exit__(None, None, None)

    from contextlib import ExitStack
    es = ExitStack()

    def psum(name, shape):
        return es.enter_context(nc.psum_tensor(name, shape, F32))

    def semaph(name):
        return es.enter_context(nc.semaphore(name))

    t1 = sb("t1", [P, DC * TT], F32)
    t2 = sb("t2", [P, DC * TT], F32)
    tb1 = sb("tb1", [P, DC * TT], BF16)
    tb2 = sb("tb2", [P, DC * TT], BF16)
    cols = sb("cols", [P, 32], F32)
    ev_r = sb("ev_r", [1, 384], F32)
    xs = sb("xs", [P, DC * TT], F32)
    bufA = sb("bufA", [P, DC * T], BF16)
    wq_s = sb("wq_s", [P, DC * D], BF16)
    wk_s = sb("wk_s", [P, DC * D], BF16)
    wv_s = sb("wv_s", [P, DC * D], BF16)
    Pf = sb("Pf", [P, DC * TT], BF16)
    Vf = sb("Vf", [P, DC * TT], BF16)
    mb = sb("mb", [P, TT], F32)
    vb = sb("vb", [P, TT], F32)
    e2b = sb("e2b", [P, TT], F32)
    nb = sb("nb", [P, TT], F32)
    ones128 = sb("ones128", [P, P], BF16)

    pm = [psum(f"pm{i}", [P, TT]) for i in range(3)]
    pst = [psum(f"pst{i}", [P, TT]) for i in range(2)]

    sm = {n: semaph(n) for n in ["qx", "qxa", "qxb", "qw", "qt", "qs", "qsg", "qr", "qg", "qo", "tv", "tsc", "ttn", "tg", "cc"]}

    C_GF = 0
    C_MAX0 = 8
    C_MIN0 = 9
    C_SC = 10

    def dma(eng, out_ap, in_ap, s):
        tgt = S.inc(s, 16)
        S.raw(eng, lambda e, o=out_ap, i=in_ap, sn=s: e.dma_start(out=o, in_=i).then_inc(sm[sn], 16))
        return tgt

    def wait(eng, s, v):
        S.q[eng].append(["wait", lambda e, sn=s, vv=v: e.wait_ge(sm[sn], vv)])

    def tick(eng, s):
        tgt = S.inc(s, 1)
        ent = S.q[eng][-1]
        if ent[0] == "inst":
            f_old = ent[1]
            ent[1] = lambda e, f=f_old, sn=s: f(e).then_inc(sm[sn], 1)
            ent[0] = "raw"
        elif ent[0] == "wait":
            S.raw(eng, lambda e, sn=s: e.sem_inc(sm[sn], 1))
        else:
            raise AssertionError(f"tick after raw op on {eng} is unordered")
        return tgt

    def vdrain():
        S.raw("vector", lambda e: e.drain())

    def red_mm(src3, pi):
        for c in range(DC):
            S.op("tensor", lambda e, c=c, s3=src3, pi=pi:
                 e.matmul(pst[pi][:], ones128[:], s3[:, c, :], start=(c == 0), stop=(c == DC - 1)))

    def recip_chunks(dst, src, n):
        for a in range(0, n, 128):
            w = min(128, n - a)
            S.op("vector", lambda e, a=a, w=w, d=dst, s=src: e.reciprocal(d[:, a:a + w], s[:, a:a + w]))

    def proj_pass(w_sb, dst, ln1_t, evac_ticks):
        for ec in range(DC):
            g = len(evac_ticks)
            bank = pm[g % 2]
            if g >= 2:
                wait("tensor", "tsc", evac_ticks[g - 2])
            for dc in range(DC):
                S.op("tensor", lambda e, b=bank, w=w_sb, dcc=dc, ecc=ec, lt=ln1_t:
                     e.matmul(b[:], w[:, dcc * D + ecc * P: dcc * D + ecc * P + P],
                              lt[:, dcc, :], start=(dcc == 0), stop=(dcc == DC - 1)))
            tt_p = tick("tensor", "ttn")
            wait("scalar", "ttn", tt_p)
            S.op("scalar", lambda e, b=bank, d=dst, ecc=ec:
                 e.copy(d[:, ecc * TT: ecc * TT + TT], b[:]))
            evac_ticks.append(tick("scalar", "tsc"))

    for w_hbm, w_sb in ((wqT, wq_s), (wkT, wk_s), (wvT, wv_s)):
        src = w_hbm.rearrange("(c p) e -> p c e", p=P)
        dma("gpsimd", w_sb[:].rearrange("p (c e) -> p c e", c=DC), src, "qw")
    S.op("vector", lambda e: e.memset(ones128[:], 1.0))
    S.op("vector", lambda e: e.memset(cols[:, 0:16], 0.0))
    S.op("vector", lambda e: e.memset(ev_r[:], 0.0))
    tick("vector", "tv")

    xs3 = xs[:].rearrange("p (c t) -> p c t", c=DC)
    t13 = t1[:].rearrange("p (c t) -> p c t", c=DC)
    t23 = t2[:].rearrange("p (c t) -> p c t", c=DC)
    tb13 = tb1[:].rearrange("p (c t) -> p c t", c=DC)
    tb23 = tb2[:].rearrange("p (c t) -> p c t", c=DC)
    Pf3 = Pf[:].rearrange("p (c t) -> p c t", c=DC)
    Vf3 = Vf[:].rearrange("p (c t) -> p c t", c=DC)
    mb_b = mb[:].unsqueeze(1).broadcast_to([P, DC, TT])
    vb_b = vb[:].unsqueeze(1).broadcast_to([P, DC, TT])

    import os as _os
    LNCUT = int(_os.environ.get("LNCUT", "99"))

    def ln_stats_apply(src_f32_3d, out_bf_t):
        vdrain()
        S.op("vector", lambda e: e.tensor_copy(tb1[:], src_f32_3d.rearrange("p c t -> p (c t)")))
        S.op("vector", lambda e, s=src_f32_3d: e.tensor_tensor(tb23, s, s, op=ALU.mult))
        tvv = tick("vector", "tv")
        wait("tensor", "tv", tvv)
        red_mm(tb13, 0)
        red_mm(tb23, 1)
        ttv = tick("tensor", "ttn")
        wait("vector", "ttn", ttv)
        S.op("vector", lambda e: e.tensor_scalar(mb[:], pst[0][:], 1.0 / D, None, op0=ALU.mult))
        S.op("vector", lambda e: e.tensor_scalar(e2b[:], pst[1][:], 1.0 / D, None, op0=ALU.mult))
        vdrain()
        S.op("vector", lambda e: e.tensor_tensor(vb[:], mb[:], mb[:], op=ALU.mult))
        vdrain()
        S.op("vector", lambda e: e.tensor_tensor(nb[:], e2b[:], vb[:], op=ALU.subtract))
        vdrain()
        S.op("vector", lambda e: e.tensor_scalar(vb[:], nb[:], EPS_LN, None, op0=ALU.add))
        tvv2 = tick("vector", "tv")
        wait("scalar", "tv", tvv2)
        S.op("scalar", lambda e: e.activation(nb[:], vb[:], AF.Sqrt))
        tsv = tick("scalar", "tsc")
        wait("vector", "tsc", tsv)
        recip_chunks(vb[:], nb[:], TT)
        vdrain()
        S.op("vector", lambda e, s=src_f32_3d: e.tensor_tensor(t13, s, mb_b, op=ALU.subtract))
        vdrain()
        S.op("vector", lambda e, o=out_bf_t: e.tensor_tensor(o, t13, vb_b, op=ALU.mult))
        return tick("vector", "tv")

    last_xs_done = 0
    for ti in range(NTILE if parts & 1 else 0):
        t0 = ti * TT
        if ti > 0:
            wait("sync", "tv", last_xs_done)
        lx = dma("sync", xs3, xT3[:, :, t0:t0 + TT], "qx")
        wait("vector", "qx", lx)
        if ti == 0:
            wait("vector", "tv", 1)
        ln1_t = bufA[:, DC * t0: DC * (t0 + TT)].rearrange("p (c t) -> p c t", c=DC)
        tv_ln1 = ln_stats_apply(xs3, ln1_t)
        last_xs_done = tv_ln1

        if ti == 0:
            wait("tensor", "qw", 16 * 3)
        wait("tensor", "tv", tv_ln1)
        evac_ticks = []
        proj_pass(wk_s, Pf, ln1_t, evac_ticks)
        proj_pass(wv_s, Vf, ln1_t, evac_ticks)
        wait("tensor", "tsc", S.val("tsc"))
        wait("vector", "tsc", S.val("tsc"))
        S.op("vector", lambda e: e.tensor_tensor(tb1[:], Pf[:], Pf[:], op=ALU.mult))
        tv_k2 = tick("vector", "tv")
        wait("tensor", "tv", tv_k2)
        red_mm(tb13, 0)
        tt_k2 = tick("tensor", "ttn")
        wait("scalar", "ttn", tt_k2)
        S.op("scalar", lambda e: e.activation(nb[:], pst[0][:], AF.Sqrt))
        ts_nrm = tick("scalar", "tsc")
        wait("vector", "tsc", ts_nrm)
        S.op("vector", lambda e: e.tensor_scalar(e2b[:], nb[:], 1e-12, None, op0=ALU.max))
        vdrain()
        recip_chunks(vb[:], e2b[:], TT)
        vdrain()
        S.op("vector", lambda e: e.tensor_tensor(t13, Pf3, vb_b, op=ALU.mult))
        vdrain()
        S.op("vector", lambda e: e.tensor_tensor(tb23, t13, Vf3, op=ALU.mult))
        vdrain()
        for c in range(DC):
            S.op("vector", lambda e, c=c:
                 e.tensor_scalar(t1[:, c * TT: c * TT + TT], tb2[:, c * TT: c * TT + TT],
                                 1.0, cols[:, C_GF + c: C_GF + c + 1], op0=ALU.mult, op1=ALU.add,
                                 accum_out=cols[:, C_GF + c: C_GF + c + 1]))
        tick("vector", "tv")

    vdrain()
    S.op("vector", lambda e: e.tensor_scalar(cols[:, 16:16 + DC], cols[:, C_GF:C_GF + DC],
                                             1.0 / T, None, op0=ALU.mult))
    tv_gf = tick("vector", "tv")
    p1_last_xs = last_xs_done if parts & 1 else 0
    if dbg >= 1:
        wait("sync", "tv", tv_gf)
        dma("sync", dbg_outs["dbg_gf"], cols[:, 16:16 + DC], "qs")

    C_GFS = 16
    last_xs_done = p1_last_xs
    for ti in range(NTILE if parts & 2 else 0):
        t0 = ti * TT
        if last_xs_done:
            wait("sync", "tv", last_xs_done)
        lx = dma("sync", xs3, xT3[:, :, t0:t0 + TT], "qx")
        ln1_t = bufA[:, DC * t0: DC * (t0 + TT)].rearrange("p (c t) -> p c t", c=DC)
        wait("vector", "qs", S.val("qs"))
        evac_ticks = []
        wait("tensor", "tv", S.val("tv"))
        proj_pass(wq_s, Pf, ln1_t, evac_ticks)
        wait("vector", "tsc", S.val("tsc"))
        S.op("vector", lambda e: e.tensor_tensor(tb1[:], Pf[:], Pf[:], op=ALU.mult))
        tv_q2 = tick("vector", "tv")
        wait("tensor", "tv", tv_q2)
        red_mm(tb13, 0)
        tt_q2 = tick("tensor", "ttn")
        wait("scalar", "ttn", tt_q2)
        S.op("scalar", lambda e: e.activation(nb[:], pst[0][:], AF.Sqrt))
        ts_nrm = tick("scalar", "tsc")
        wait("vector", "tsc", ts_nrm)
        S.op("vector", lambda e: e.tensor_scalar(e2b[:], nb[:], 1e-12, None, op0=ALU.max))
        vdrain()
        recip_chunks(vb[:], e2b[:], TT)
        vdrain()
        wait("vector", "qx", lx)
        for c in range(DC):
            S.op("vector", lambda e, c=c:
                 e.scalar_tensor_tensor(t1[:, c * TT: c * TT + TT], Pf[:, c * TT: c * TT + TT],
                                        cols[:, C_GFS + c: C_GFS + c + 1], vb[:],
                                        op0=ALU.mult, op1=ALU.mult))
        vdrain()
        S.op("vector", lambda e: e.tensor_tensor(t2[:], t1[:], xs[:], op=ALU.add))
        tv_x1 = tick("vector", "tv")
        last_xs_done = tv_x1
        wait("sync", "tv", tv_x1)
        st_x1 = dma("sync", x13[:, :, t0:t0 + TT], t23, "qs")
        if dbg >= 1:
            st_x1 = dma("sync", dbg_outs["dbg_x1"][:, DC * t0: DC * (t0 + TT)], t2[:], "qs")
        ln2_tile = bufA[:, DC * t0: DC * (t0 + TT)].rearrange("p (c t) -> p c t", c=DC)
        tv_ln2 = ln_stats_apply(t23, ln2_tile)
        wait("sync", "tv", tv_ln2)
        dma("sync", ln2h3[:, :, t0:t0 + TT], ln2_tile, "qs")
        vdrain()
        lnflat = bufA[:, DC * t0: DC * (t0 + TT)]
        S.op("vector", lambda e, s=lnflat: e.tensor_scalar(Pf[:], s, 1.0, cols[:, C_MAX0:C_MAX0 + 1],
                                                           op0=ALU.mult, op1=ALU.max,
                                                           accum_out=cols[:, C_MAX0:C_MAX0 + 1]))
        S.op("vector", lambda e, s=lnflat: e.tensor_scalar(Vf[:], s, -1.0, cols[:, C_MIN0:C_MIN0 + 1],
                                                           op0=ALU.mult, op1=ALU.max,
                                                           accum_out=cols[:, C_MIN0:C_MIN0 + 1]))
        tick("vector", "tv")

    def scalar_event(col_start, k, n_out, scalar_chain, out_col=C_SC):
        wait("sync", "tv", S.val("tv"))
        for j in range(k):
            dma("sync", s_scal[j * P:(j + 1) * P], cols[:, col_start + j:col_start + j + 1], "qr")
        wait("sync", "qr", S.val("qr"))
        l2 = dma("sync", ev_r[:, 0:P * k], s_scal[0:P * k].unsqueeze(0), "qr")
        wait("vector", "qr", l2)
        vdrain()
        for j in range(k):
            S.op("vector", lambda e, j=j:
                 e.tensor_reduce(ev_r[:, 264 + j:265 + j], ev_r[:, j * P:(j + 1) * P],
                                 axis=mybir.AxisListType.X, op=ALU.max))
        tvv = tick("vector", "tv")
        wait("sync", "tv", tvv)
        l3 = dma("sync", cc_in[0:8], ev_r[:, 264:264 + 8], "qr")
        wait("gpsimd", "qr", l3)
        cctarget = S.inc("cc", 1)
        S.raw("gpsimd", lambda e: e.collective_compute(
            "AllGather", ALU.bypass, replica_groups=[list(range(8))],
            ins=[cc_in.opt()], outs=[cc_out.opt()]).then_inc(sm["cc"], 1))
        S.raw("gpsimd", lambda e, v=cctarget: e.wait_ge(sm["cc"], v))
        l4 = dma("gpsimd", ev_r[:, 0:64], cc_out.unsqueeze(0), "qg")
        wait("vector", "qg", l4)
        vdrain()
        for j in range(k):
            view = ev_r[:, 0:64].rearrange("a (r q) -> a q r", q=8)
            S.op("vector", lambda e, j=j, v=view:
                 e.tensor_reduce(ev_r[:, 272 + j:273 + j], v[:, j, :],
                                 axis=mybir.AxisListType.X, op=ALU.max))
        scalar_chain()
        tvv = tick("vector", "tv")
        wait("sync", "tv", tvv)
        l5 = dma("sync", s_scal[128:128 + n_out], ev_r[:, 280:280 + n_out], "qr")
        wait("sync", "qr", l5)
        l6 = dma("sync", cols[:, out_col:out_col + n_out],
                 s_scal[128:128 + n_out].unsqueeze(0).broadcast_to([P, n_out]), "qr")
        wait("vector", "qr", l6)
        return tick("vector", "tv")

    S.op("vector", lambda e: e.memset(ev_r[:, 300:301], QMAX))
    vdrain()
    S.op("vector", lambda e: e.reciprocal(ev_r[:, 301:302], ev_r[:, 300:301]))

    def chain_e0():
        vdrain()
        S.op("vector", lambda e: e.tensor_tensor(ev_r[:, 324:325], ev_r[:, 272:273],
                                                 ev_r[:, 301:302], op=ALU.mult))
        vdrain()
        S.op("vector", lambda e: e.tensor_scalar(ev_r[:, 280:281], ev_r[:, 324:325],
                                                 1e-12, None, op0=ALU.max))
        vdrain()
        S.op("vector", lambda e: e.reciprocal(ev_r[:, 281:282], ev_r[:, 280:281]))

    vdrain()
    S.op("vector", lambda e: e.tensor_tensor(cols[:, 24:25], cols[:, C_MAX0:C_MAX0 + 1],
                                             cols[:, C_MIN0:C_MIN0 + 1], op=ALU.max))
    tick("vector", "tv")
    vdrain()
    tv_e0 = scalar_event(24, 1, 2, chain_e0) if parts & 4 else tick("vector", "tv")

    if dbg >= 1:
        wait("sync", "tv", tv_e0)
        dma("sync", dbg_outs["dbg_scal"], ev_r[:, 272:288], "qs")
        dma("sync", dbg_outs["dbg_ln2"], bufA[:], "qs")


    if parts & 8:
        HF = 2048

        sbfree("ones128", "nb", "e2b", "vb", "mb", "Vf", "Pf", "wv_s", "wk_s", "wq_s", "bufA", "xs")
        wB = sb("wB", [P, DC * I], BF16)
        wA = sb("wA", [P, DC * I], BF16)
        DQW = 1024
        dq_w = sb("dq_w", [P, DQW], F32)
        dq_m = sb("dq_m", [P, 48], F32)
        dq_s = sb("dq_s", [P, 48], F32)
        dq_r = sb("dq_r", [P, 48], F32)
        dq_q = sb("dq_q", [P, DQW], I16)
        dq_f = sb("dq_f", [P, DQW], F32)
        dq_o = sb("dq_o", [P, DQW], BF16)
        stg = sb("stg", [P, 2 * 4 * TT], BF16)
        g_inA = sb("g_inA", [P, HF], BF16)
        g_inB = sb("g_inB", [P, HF], BF16)
        u_inA = sb("u_inA", [P, HF], BF16)
        u_inB = sb("u_inB", [P, HF], BF16)
        n1_t = sb("n1_t", [P, HF], I16)
        z_t = sb("z_t", [P, HF], F32)
        r_t = sb("r_t", [P, HF], U16)
        G_t = sb("G_t", [P, HF], BF16)

        C_MAXG, C_MING, C_MAXU, C_MINU = 0, 1, 2, 3
        C_MAXG3, C_MING3, C_MAXC, C_MINC = 4, 5, 6, 7
        wait("vector", "qs", S.val("qs"))
        S.op("vector", lambda e: e.memset(cols[:, 0:8], 0.0))
        S.op("vector", lambda e: e.memset(cols[:, 25:26], 7.0))
        vdrain()
        S.op("vector", lambda e: e.reciprocal(cols[:, 26:27], cols[:, 25:26]))
        tick("vector", "tv")

        def dequant_weight(w_hbm, rows_total, cols_w, s_out):
            for r0 in range(0, rows_total, P):
                for c0 in range(0, cols_w, DQW):
                    cw = min(DQW, cols_w - c0)
                    nbk = cw // 32
                    ld = dma("sync", dq_w[:, 0:cw], w_hbm[r0:r0 + P, c0:c0 + cw], "qx")
                    wait("vector", "qx", ld)
                    vdrain()
                    w3 = dq_w[:, 0:cw].rearrange("p (b k) -> p b k", k=32)
                    S.op("vector", lambda e, w3=w3, nbk=nbk:
                         e.tensor_reduce(dq_m[:, 0:nbk], w3, axis=mybir.AxisListType.X,
                                         op=ALU.max, apply_absolute_value=True))
                    vdrain()
                    S.op("vector", lambda e, nbk=nbk: e.tensor_scalar(
                        dq_s[:, 0:nbk], dq_m[:, 0:nbk], cols[:, 26:27], 1e-12,
                        op0=ALU.mult, op1=ALU.max))
                    vdrain()
                    recip_chunks(dq_r[:, 0:nbk], dq_s[:, 0:nbk], nbk)
                    vdrain()
                    mb_c = dq_r[:, 0:nbk].unsqueeze(2).broadcast_to([P, nbk, 32])
                    sb_c = dq_s[:, 0:nbk].unsqueeze(2).broadcast_to([P, nbk, 32])
                    f3 = dq_f[:, 0:cw].rearrange("p (b k) -> p b k", k=32)
                    q3 = dq_q[:, 0:cw].rearrange("p (b k) -> p b k", k=32)
                    o3 = dq_o[:, 0:cw].rearrange("p (b k) -> p b k", k=32)
                    S.op("vector", lambda e, w3=w3, m=mb_c, f3=f3: e.tensor_tensor(f3, w3, m, op=ALU.mult))
                    vdrain()
                    S.op("vector", lambda e, cw=cw: e.tensor_scalar(
                        dq_q[:, 0:cw], dq_f[:, 0:cw], 1.0, None, op0=ALU.mult))
                    vdrain()
                    S.op("vector", lambda e, q3=q3, s=sb_c, o3=o3: e.tensor_tensor(o3, q3, s, op=ALU.mult))
                    tvq = tick("vector", "tv")
                    wait("sync", "tv", tvq)
                    ost = dma("sync",
                              s_out[c0 // P:(c0 + cw) // P, r0:r0 + P, :].rearrange("c p k -> p c k"),
                              dq_o[:, 0:cw].rearrange("p (c k) -> p c k", k=P), "qs")
                    wait("vector", "qs", ost)

        dequant_weight(wg_h, I, D, s_wgT)
        dequant_weight(wu_h, I, D, s_wuT)
        dequant_weight(wd_h, D, I, s_wdT)

        def transpose_batch(pairs):
            for gi in range(0, len(pairs), 4):
                for out_ap, in_ap in pairs[gi:gi + 4]:
                    S.inc("qt", 16)
                    S.raw("sync", lambda e, o=out_ap, i=in_ap:
                          e.dma_start_transpose(out=o, in_=i).then_inc(sm["qt"], 16))
                wait("sync", "qt", S.val("qt"))
            return S.val("qt")

        qt_wg = transpose_batch([(wA[:, ch * I:(ch + 1) * I], s_wgT[ch]) for ch in range(DC)])
        qw_ffn = transpose_batch([(wB[:, ch * I:(ch + 1) * I], s_wuT[ch]) for ch in range(DC)])

        def ffn_proj(w_sb, s_dst3, maxcol, mincol, sval0):
            evt = {}
            last_mm = 0
            for ti in range(NTILE):
                t0 = ti * TT
                if ti > 0:
                    wait("sync", "ttn", last_mm)
                lt = dma("sync", tb13, ln2h3[:, :, t0:t0 + TT], "qx")
                wait("vector", "qx", lt)
                wait("vector", "ttn", last_mm)
                vdrain()
                t1i = t1[:].bitcast(I16)[:, 0:DC * TT]
                S.op("vector", lambda e, t1i=t1i: e.tensor_scalar(
                    t1i, tb1[:], cols[:, 11:12], None, op0=ALU.mult))
                vdrain()
                S.op("vector", lambda e, t1i=t1i: e.tensor_copy(tb2[:], t1i))
                tvq = tick("vector", "tv")
                wait("tensor", "tv", tvq)
                for ic in range(IC):
                    g = ti * IC + ic
                    bank = pm[g % 3]
                    if g >= 3:
                        wait("tensor", "tv", evt[g - 3])
                    for dc in range(DC):
                        S.op("tensor", lambda e, b=bank, w=w_sb, dcc=dc, icc=ic:
                             e.matmul(b[:], w[:, dcc * I + icc * P: dcc * I + icc * P + P],
                                      tb23[:, dcc, :], start=(dcc == 0), stop=(dcc == DC - 1)))
                    ttp = tick("tensor", "ttn")
                    last_mm = ttp
                    wait("vector", "ttn", ttp)
                    slot = g % 8
                    stgs = stg[:, slot * TT:(slot + 1) * TT]
                    wait("vector", "qs", S.val("qs"))
                    S.op("vector", lambda e, b=bank, ss=stgs, mc=maxcol:
                         e.tensor_scalar(ss, b[:], cols[:, sval0:sval0 + 1], cols[:, mc:mc + 1],
                                         op0=ALU.mult, op1=ALU.max,
                                         accum_out=cols[:, mc:mc + 1]))
                    vdrain()
                    S.op("vector", lambda e, ss=stgs, mc=mincol:
                         e.tensor_scalar(t2[:, 0:TT], ss, -1.0, cols[:, mc:mc + 1],
                                         op0=ALU.mult, op1=ALU.max,
                                         accum_out=cols[:, mc:mc + 1]))
                    evt[g] = tick("vector", "tv")
                    if slot == 3 or slot == 7:
                        wait("sync", "tv", evt[g])
                        dma("sync", s_dst3[:, ic - 3:ic + 1, t0:t0 + TT],
                            stg[:, (slot - 3) * TT:(slot + 1) * TT].rearrange(
                                "p (c t) -> p c t", c=4), "qs")
            return evt

        wait("tensor", "qt", qt_wg)
        wait("sync", "qs", S.val("qs"))
        ffn_proj(wA, sg3, C_MAXG, C_MING, 10)

        vdrain()
        S.op("vector", lambda e: e.tensor_tensor(cols[:, 16:17], cols[:, C_MAXG:C_MAXG + 1],
                                                 cols[:, C_MING:C_MING + 1], op=ALU.max))
        tick("vector", "tv")
        vdrain()

        def chain_e2():
            vdrain()
            S.op("vector", lambda e: e.tensor_tensor(ev_r[:, 324:325], ev_r[:, 272:273],
                                                     ev_r[:, 301:302], op=ALU.mult))
            vdrain()
            S.op("vector", lambda e: e.tensor_scalar(ev_r[:, 280:281], ev_r[:, 324:325],
                                                     1e-12, None, op0=ALU.max))
            vdrain()
            S.op("vector", lambda e: e.reciprocal(ev_r[:, 281:282], ev_r[:, 280:281]))
            vdrain()
            S.op("vector", lambda e: e.tensor_copy(ev_r[:, 310:311], ev_r[:, 280:281]))

        scalar_event(16, 1, 2, chain_e2, out_col=12)

        wait("tensor", "qt", qw_ffn)

        ffn_proj(wB, su3, C_MAXU, C_MINU, 10)

        def pass4a():
            wait("sync", "qs", S.val("qs"))
            nu = IC * (T // HF)
            last_use = {}
            for u in range(nu):
                ic, half = u // 2, u % 2
                h0 = half * HF
                gin = g_inA if u % 2 == 0 else g_inB
                qsem = "qxa" if u % 2 == 0 else "qxb"
                if u >= 2:
                    wait("sync", "tv", last_use[u - 2])
                lg = dma("sync", gin[:], s_gate[ic * P:(ic + 1) * P, h0:h0 + HF], qsem)
                wait("vector", qsem, lg)
                vdrain()
                S.op("vector", lambda e, g=gin: e.tensor_scalar(
                    n1_t[:], g[:], cols[:, 13:14], None, op0=ALU.mult))
                tvn = tick("vector", "tv")
                last_use[u] = tvn
                wait("scalar", "tv", tvn)
                S.op("scalar", lambda e: e.activation(z_t[:], n1_t[:], AF.Sigmoid,
                                                      scale=cols[:, 12:13]))
                tsz = tick("scalar", "tsc")
                wait("vector", "tsc", tsz)
                S.op("vector", lambda e: e.tensor_scalar(r_t[:], z_t[:], 65536.0, None, op0=ALU.mult))
                vdrain()
                wait("vector", "qsg", S.val("qsg"))
                S.op("vector", lambda e: e.tensor_tensor(G_t[:], n1_t[:], r_t[:], op=ALU.mult))
                vdrain()
                S.op("vector", lambda e: e.tensor_scalar(z_t[:], G_t[:], 1.0, cols[:, C_MAXG3:C_MAXG3 + 1],
                                                         op0=ALU.mult, op1=ALU.max,
                                                         accum_out=cols[:, C_MAXG3:C_MAXG3 + 1]))
                S.op("vector", lambda e: e.tensor_scalar(n1_t[:], G_t[:], -1.0, cols[:, C_MING3:C_MING3 + 1],
                                                         op0=ALU.mult, op1=ALU.max,
                                                         accum_out=cols[:, C_MING3:C_MING3 + 1]))
                tvG = tick("vector", "tv")
                wait("sync", "tv", tvG)
                dma("sync", s_G[ic * P:(ic + 1) * P, h0:h0 + HF], G_t[:], "qsg")

        pass4a()

        vdrain()
        S.op("vector", lambda e: e.tensor_tensor(cols[:, 16:17], cols[:, C_MAXU:C_MAXU + 1],
                                                 cols[:, C_MINU:C_MINU + 1], op=ALU.max))
        S.op("vector", lambda e: e.tensor_tensor(cols[:, 17:18], cols[:, C_MAXG3:C_MAXG3 + 1],
                                                 cols[:, C_MING3:C_MING3 + 1], op=ALU.max))
        tick("vector", "tv")
        vdrain()

        def chain_e13():
            vdrain()
            S.op("vector", lambda e: e.tensor_tensor(ev_r[:, 324:325], ev_r[:, 272:273],
                                                     ev_r[:, 301:302], op=ALU.mult))
            vdrain()
            S.op("vector", lambda e: e.tensor_scalar(ev_r[:, 312:313], ev_r[:, 324:325],
                                                     1e-12, None, op0=ALU.max))
            vdrain()
            S.op("vector", lambda e: e.reciprocal(ev_r[:, 280:281], ev_r[:, 312:313]))
            vdrain()
            S.op("vector", lambda e: e.tensor_tensor(ev_r[:, 322:323], ev_r[:, 273:274],
                                                     ev_r[:, 310:311], op=ALU.mult))
            vdrain()
            S.op("vector", lambda e: e.tensor_scalar(ev_r[:, 323:324], ev_r[:, 322:323],
                                                     1.0 / 65536.0, None, op0=ALU.mult))
            vdrain()
            S.op("vector", lambda e: e.tensor_tensor(ev_r[:, 322:323], ev_r[:, 323:324],
                                                     ev_r[:, 301:302], op=ALU.mult))
            vdrain()
            S.op("vector", lambda e: e.tensor_scalar(ev_r[:, 311:312], ev_r[:, 322:323],
                                                     1e-12, None, op0=ALU.max))
            vdrain()
            S.op("vector", lambda e: e.reciprocal(ev_r[:, 321:322], ev_r[:, 311:312]))
            vdrain()
            S.op("vector", lambda e: e.tensor_tensor(ev_r[:, 320:321], ev_r[:, 310:311],
                                                     ev_r[:, 321:322], op=ALU.mult))
            vdrain()
            S.op("vector", lambda e: e.tensor_scalar(ev_r[:, 281:282], ev_r[:, 320:321],
                                                     1.0 / 65536.0, None, op0=ALU.mult))

        scalar_event(16, 2, 2, chain_e13, out_col=14)

        def pass4b():
            wait("sync", "qs", S.val("qs"))
            wait("sync", "qsg", S.val("qsg"))
            nu = IC * (T // HF)
            last_use = {}
            for u in range(nu):
                ic, half = u // 2, u % 2
                h0 = half * HF
                gin = g_inA if u % 2 == 0 else g_inB
                uin = u_inA if u % 2 == 0 else u_inB
                qsem = "qxa" if u % 2 == 0 else "qxb"
                if u >= 2:
                    wait("sync", "tv", last_use[u - 2])
                lg = dma("sync", gin[:], s_G[ic * P:(ic + 1) * P, h0:h0 + HF], qsem)
                lu = dma("sync", uin[:], s_up[ic * P:(ic + 1) * P, h0:h0 + HF], qsem)
                wait("vector", qsem, lu)
                vdrain()
                S.op("vector", lambda e, g=gin: e.tensor_scalar(
                    n1_t[:], g[:], cols[:, 15:16], None, op0=ALU.mult))
                S.op("vector", lambda e, uu=uin: e.tensor_scalar(
                    r_t[:].bitcast(I16), uu[:], cols[:, 14:15], None, op0=ALU.mult))
                last_use[u] = tick("vector", "tv")
                vdrain()
                wait("vector", "qsg", S.val("qsg"))
                S.op("vector", lambda e: e.tensor_tensor(G_t[:], n1_t[:], r_t[:].bitcast(I16), op=ALU.mult))
                vdrain()
                S.op("vector", lambda e: e.tensor_scalar(z_t[:], G_t[:], 1.0, cols[:, C_MAXC:C_MAXC + 1],
                                                         op0=ALU.mult, op1=ALU.max,
                                                         accum_out=cols[:, C_MAXC:C_MAXC + 1]))
                S.op("vector", lambda e: e.tensor_scalar(n1_t[:], G_t[:], -1.0, cols[:, C_MINC:C_MINC + 1],
                                                         op0=ALU.mult, op1=ALU.max,
                                                         accum_out=cols[:, C_MINC:C_MINC + 1]))
                tvG = tick("vector", "tv")
                wait("sync", "tv", tvG)
                dma("sync", s_c[ic * P:(ic + 1) * P, h0:h0 + HF], G_t[:], "qsg")

        pass4b()

        qw_wd = transpose_batch([(wB[:, ch * D:(ch + 1) * D], s_wdT[ch]) for ch in range(IC)])

        vdrain()
        S.op("vector", lambda e: e.tensor_tensor(cols[:, 16:17], cols[:, C_MAXC:C_MAXC + 1],
                                                 cols[:, C_MINC:C_MINC + 1], op=ALU.max))
        tick("vector", "tv")
        vdrain()

        def chain_e4():
            vdrain()
            S.op("vector", lambda e: e.tensor_tensor(ev_r[:, 322:323], ev_r[:, 272:273],
                                                     ev_r[:, 311:312], op=ALU.mult))
            vdrain()
            S.op("vector", lambda e: e.tensor_tensor(ev_r[:, 323:324], ev_r[:, 322:323],
                                                     ev_r[:, 312:313], op=ALU.mult))
            vdrain()
            S.op("vector", lambda e: e.tensor_tensor(ev_r[:, 322:323], ev_r[:, 323:324],
                                                     ev_r[:, 301:302], op=ALU.mult))
            vdrain()
            S.op("vector", lambda e: e.tensor_scalar(ev_r[:, 281:282], ev_r[:, 322:323],
                                                     1e-12, None, op0=ALU.max))
            vdrain()
            S.op("vector", lambda e: e.reciprocal(ev_r[:, 321:322], ev_r[:, 281:282]))
            vdrain()
            S.op("vector", lambda e: e.tensor_tensor(ev_r[:, 320:321], ev_r[:, 311:312],
                                                     ev_r[:, 312:313], op=ALU.mult))
            vdrain()
            S.op("vector", lambda e: e.tensor_tensor(ev_r[:, 280:281], ev_r[:, 320:321],
                                                     ev_r[:, 321:322], op=ALU.mult))

        scalar_event(16, 1, 2, chain_e4, out_col=18)

        sbfree("G_t", "r_t", "z_t", "n1_t", "u_inB", "u_inA", "g_inB", "g_inA", "stg",
               "dq_o", "dq_f", "dq_q", "dq_r", "dq_s", "dq_m", "dq_w", "wA")
        cin = sb("cin", [P, IC * TT], BF16)
        oqi = sb("oqi", [P, IC * TT], I16)
        oqb = sb("oqb", [P, IC * TT], BF16)

        wait("tensor", "qt", qw_wd)
        wait("sync", "qs", S.val("qs"))
        wait("sync", "qsg", S.val("qsg"))
        last_st = 0
        for ti in range(NTILE):
            t0 = ti * TT
            wait("vector", "qo", last_st)
            lc = dma("sync", cin[:].rearrange("p (c t) -> p c t", c=IC), sc3[:, :, t0:t0 + TT], "qx")
            lx1 = dma("sync", t13, x13[:, :, t0:t0 + TT], "qx")
            wait("vector", "qx", lx1)
            vdrain()
            S.op("vector", lambda e: e.tensor_scalar(oqi[:], cin[:], cols[:, 18:19], None, op0=ALU.mult))
            vdrain()
            S.op("vector", lambda e: e.tensor_copy(oqb[:], oqi[:]))
            tvo = tick("vector", "tv")
            wait("tensor", "tv", tvo)
            oq3 = oqb[:].rearrange("p (c t) -> p c t", c=IC)
            evh = {}
            for hc in range(DC):
                bank = pm[hc % 3]
                if hc >= 3:
                    wait("tensor", "tv", evh[hc - 3])
                for ic in range(IC):
                    S.op("tensor", lambda e, b=bank, icc=ic, hcc=hc:
                         e.matmul(b[:], wB[:, icc * D + hcc * P: icc * D + hcc * P + P],
                                  oq3[:, icc, :], start=(icc == 0), stop=(icc == IC - 1)))
                tth = tick("tensor", "ttn")
                wait("vector", "ttn", tth)
                S.op("vector", lambda e, b=bank, hcc=hc:
                     e.scalar_tensor_tensor(t2[:, hcc * TT:(hcc + 1) * TT], b[:],
                                            cols[:, 19:20], t1[:, hcc * TT:(hcc + 1) * TT],
                                            op0=ALU.mult, op1=ALU.add))
                evh[hc] = tick("vector", "tv")
            wait("sync", "tv", evh[DC - 1])
            last_st = dma("sync", out3[:, :, t0:t0 + TT], t23, "qo")

    wait("sync", "qs", S.val("qs"))
    wait("sync", "qr", S.val("qr"))
    wait("sync", "qo", S.val("qo"))

    blk = es.enter_context(nc.Block())
    with es:
        for eng_name in Sched.ENGINES:
            deco = getattr(blk, eng_name)

            def make(eng_ops):
                def body(e):
                    for kind, f in eng_ops:
                        f(e)
                return body
            deco(make(S.q[eng_name]))
    return nc


def kernel_parts(parts, L0_attn_x, L0_attn_W_q, L0_attn_W_k, L0_attn_W_v,
                 L0_ffn_w_gate, L0_ffn_w_up, L0_ffn_w_down, dbg=0):
    nc = build(dbg=dbg, parts=parts)
    in_maps = []
    for c in range(8):
        in_maps.append(dict(
            xT=np.ascontiguousarray(L0_attn_x[c].T).astype(np.float32),
            wqT=np.ascontiguousarray(L0_attn_W_q.T).astype(np.float32),
            wkT=np.ascontiguousarray(L0_attn_W_k.T).astype(np.float32),
            wvT=np.ascontiguousarray(L0_attn_W_v.T).astype(np.float32),
            wg=np.ascontiguousarray(L0_ffn_w_gate).astype(np.float32),
            wu=np.ascontiguousarray(L0_ffn_w_up).astype(np.float32),
            wd=np.ascontiguousarray(L0_ffn_w_down).astype(np.float32),
        ))
    res = run_bass_kernel_spmd(nc, in_maps, core_ids=list(range(8)))
    if dbg:
        return res
    outs = [np.ascontiguousarray(r["out"].T) for r in res.results]
    return np.stack(outs).astype(np.float32)


def kernel(**kw):
    return kernel_parts(15, **kw)
